# version 01
# baseline (speedup 1.0000x reference)
import numpy as np
import ml_dtypes
from contextlib import ExitStack
import concourse.bass as bass
import concourse.mybir as mybir
from concourse.bass_utils import run_bass_kernel_spmd

F32 = mybir.dt.float32
BF16 = mybir.dt.bfloat16
AF = mybir.ActivationFunctionType
ALU = mybir.AluOpType

T = 2048
D = 1024
K = 8
CP = 512
NCH = T // CP
XNW = 2056
QB = 128.0
EPS = 1e-5
CLIPB = float(np.float32(QB) - np.float32(EPS))
MAGIC = float(np.float32(3 * 2 ** 22))

ENGS = ("sync", "gpsimd", "tensor", "vector", "scalar")


def build_program(debug=False, serial=0):
    nc = bass.Bass("TRN2", target_bir_lowering=False)

    xT = nc.dram_tensor("xT", [D, T], F32, kind="ExternalInput").ap()
    wwT = nc.dram_tensor("wwT", [D, D], F32, kind="ExternalInput").ap()
    wrT = nc.dram_tensor("wrT", [D, D], F32, kind="ExternalInput").ap()
    wf = nc.dram_tensor("wf", [D, D], F32, kind="ExternalInput").ap()
    gvec = nc.dram_tensor("gvec", [1, D], F32, kind="ExternalInput").ap()
    bvec = nc.dram_tensor("bvec", [1, D], F32, kind="ExternalInput").ap()
    scal_in = nc.dram_tensor("scal_in", [1, 8], F32, kind="ExternalInput").ap()
    ETd = nc.dram_tensor("ETd", [CP, CP], F32, kind="ExternalInput").ap()
    CTd = nc.dram_tensor("CTd", [CP, CP], BF16, kind="ExternalInput").ap()
    IOTAd = nc.dram_tensor("IOTAd", [1, 2 * CP], F32, kind="ExternalInput").ap()
    IDENTd = nc.dram_tensor("IDENTd", [128, 128], BF16, kind="ExternalInput").ap()
    ONESd = nc.dram_tensor("ONESd", [128, 1], BF16, kind="ExternalInput").ap()

    yT = nc.dram_tensor("yT", [D, T], F32, kind="ExternalOutput").ap()

    hrowR = nc.dram_tensor("hrowR", [1, T], F32)
    hrowQ = nc.dram_tensor("hrowQ", [1, T], F32)
    hrowD = nc.dram_tensor("hrowD", [1, CP], F32)
    hscal = nc.dram_tensor("hscal", [1, 8], F32)
    wqh = nc.dram_tensor("wqh", [128, K * D], BF16)
    cc_in1 = nc.dram_tensor("cc_in1", [1, 1], F32)
    cc_out1 = nc.dram_tensor("cc_out1", [1, 1], F32)
    cc_in2 = nc.dram_tensor("cc_in2", [1, 1], F32)
    cc_out2 = nc.dram_tensor("cc_out2", [1, 1], F32)

    es = ExitStack()
    sbt = lambda name, shape, dt: es.enter_context(nc.sbuf_tensor(name, shape, dt))
    pst = lambda name, shape, dt: es.enter_context(nc.psum_tensor(name, shape, dt))

    X = sbt("X", [128, K * T], F32)
    XN = sbt("XN", [128, K * XNW], BF16)
    U1 = sbt("U1", [128, 2048], F32)
    RBT = sbt("RBT", [128, T], F32)
    STB = sbt("STB", [128, K * D], BF16)
    WA = sbt("WA", [128, K * D], BF16)
    WB = sbt("WB", [128, K * D], BF16)
    VT = sbt("VT", [128, 4 * D], BF16)
    WKT = sbt("WKT", [128, 4 * D], BF16)
    SMB = sbt("SMB", [128, 4 * CP], BF16)
    RDS = sbt("RDS", [128, K * CP], BF16)
    RKS = sbt("RKS", [128, K * CP], BF16)
    MT = sbt("MT", [128, 4 * CP], BF16)
    DEC = sbt("DEC", [128, CP], BF16)
    G8 = sbt("G8", [128, K], F32)
    B8 = sbt("B8", [128, K], F32)
    GWC = sbt("GWC", [128, 4], F32)
    SCAL = sbt("SCAL", [128, 32], F32)
    PMX = sbt("PMX", [128, 32], F32)
    IDT = sbt("IDT", [128, 128], BF16)
    ONE = sbt("ONE", [128, 1], BF16)
    LMX = sbt("LMX", [128, 1], BF16)
    WFT = sbt("WFT", [128, D], F32)

    pA = pst("pA", [128, 512], F32)
    pB = pst("pB", [128, 512], F32)
    pC = pst("pC", [128, 1024], F32)
    pD = pst("pD", [128, 1024], F32)
    pT = pst("pT", [128, 1024], BF16)
    pS = pst("pS", [1, 512], F32)

    sems = {}
    for nm in ("L", "G", "T", "V", "A", "C", "S", "GE"):
        sems[nm] = es.enter_context(nc.semaphore("sem" + nm))
    NPOOL = 48
    for i in range(NPOOL):
        sems["D%d" % i] = es.enter_context(nc.semaphore("semD%d" % i))

    ops = {e: [] for e in ENGS}
    cnt = {nm: 0 for nm in sems}
    dma_ev = {}
    pool_state = {"idx": 0}

    def OP(eng, builder, inc=None, n=1):
        if inc in ("L", "G") and n == 16:
            i = pool_state["idx"]
            pool_state["idx"] = (i + 1) % NPOOL
            pname = "D%d" % i
            cnt[pname] += 16
            pval = cnt[pname]
            cnt[inc] += 16
            val = cnt[inc]
            dma_ev[(inc, val)] = (pname, pval)
            def fn(e, b=builder, pname=pname):
                b(e).then_inc(sems[pname], 16)
            ops[eng].append(fn)
            return val
        if inc is not None:
            cnt[inc] += n
        val = cnt[inc] if inc is not None else None
        def fn(e, b=builder, inc=inc, n=n):
            ins = b(e)
            if inc is not None:
                ins.then_inc(sems[inc], n)
        ops[eng].append(fn)
        return val

    def WAIT(eng, sem, val):
        if val is None or val <= 0:
            return
        if (sem, val) in dma_ev:
            sem, val = dma_ev[(sem, val)]
        elif sem in ("L", "G"):
            todo = [("D%d" % i, cnt["D%d" % i]) for i in range(NPOOL) if cnt["D%d" % i] > 0]
            for sm, vv in todo:
                def fn2(e, sm=sm, vv=vv):
                    e.wait_ge(sems[sm], vv)
                ops[eng].append(fn2)
            return
        def fn(e, sem=sem, val=val):
            e.wait_ge(sems[sem], val)
        ops[eng].append(fn)

    dbg_outs = {}
    def tap(name, ap_fn, waits):
        if not debug:
            return
        shape = list(ap_fn().shape)
        dt_ = ap_fn().dtype
        dout = nc.dram_tensor("dbg_" + name, shape, dt_, kind="ExternalOutput").ap()
        dbg_outs[name] = True
        for eng, sem, val in waits:
            WAIT("sync", sem, val)
        sv = OP("sync", lambda e: e.dma_start(out=dout, in_=ap_fn()), inc="S", n=16)
        for eng in ENGS:
            WAIT(eng, "S", sv)

    Xap = X.ap()
    XNap = XN.ap()
    X3 = Xap.rearrange("p (k t) -> p k t", k=K)
    XN3 = XNap.rearrange("p (k t) -> p k t", k=K)
    U13 = U1.ap().rearrange("p (k t) -> p k t", k=4)
    RKS3 = RKS.ap().rearrange("p (k t) -> p k t", k=K)
    R0 = SCAL.ap()

    def xsl(k, c0, w):
        return Xap[:, T * k + c0: T * k + c0 + w]

    def x3(k0, kn, c0, w):
        return X3[:, k0:k0 + kn, c0:c0 + w]

    def xnsl(k, c0, w):
        return XNap[:, XNW * k + 4 + c0: XNW * k + 4 + c0 + w]

    def xn3(k0, kn, c0, w):
        return XN3[:, k0:k0 + kn, 4 + c0: 4 + c0 + w]

    def bc(ap2d, kn):
        return ap2d.unsqueeze(1).broadcast_to([ap2d.shape[0], kn, ap2d.shape[1]])

    ev = {}

    ev["x"] = []
    for k in range(K):
        ev["x"].append(OP("sync", lambda e, k=k: e.dma_start(out=xsl(k, 0, T), in_=xT[128 * k:128 * (k + 1), :]), inc="L", n=16))
    for k in range(K):
        OP("sync", lambda e, k=k: e.dma_start(out=G8.ap()[:, k:k + 1], in_=gvec[0:1, 128 * k:128 * (k + 1)].rearrange("a b -> b a")), inc="L", n=16)
    for k in range(K):
        OP("sync", lambda e, k=k: e.dma_start(out=B8.ap()[:, k:k + 1], in_=bvec[0:1, 128 * k:128 * (k + 1)].rearrange("a b -> b a")), inc="L", n=16)
    OP("sync", lambda e: e.dma_start(out=IDT[:, :], in_=IDENTd[:, :]), inc="L", n=16)
    OP("sync", lambda e: e.dma_start(out=ONE[:, :], in_=ONESd[:, :]), inc="L", n=16)
    ev["consts"] = OP("sync", lambda e: e.dma_start(out=R0[0:1, 0:8], in_=scal_in[:, :]), inc="L", n=16)
    for k in range(K):
        ev["wwT"] = OP("gpsimd", lambda e, k=k: e.dma_start(out=WB.ap()[:, D * k:D * (k + 1)], in_=wwT[128 * k:128 * (k + 1), :]), inc="G", n=16)
    WAIT("vector", "L", ev["consts"])
    for k in range(K):
        last = OP("vector", lambda e, k=k: e.memset(XNap[:, XNW * k + 3: XNW * k + 4], 0.0), inc="V" if k == K - 1 else None)
    ev["xnz"] = last

    WAIT("scalar", "L", ev["consts"])
    OP("scalar", lambda e: e.activation(R0[0:1, 8:9], R0[0:1, 0:1], AF.Exp, scale=-1.0))
    aal = OP("scalar", lambda e: e.activation(R0[0:1, 11:12], R0[0:1, 1:2], AF.Exp), inc="A")
    WAIT("vector", "A", aal)
    vrc = OP("vector", lambda e: e.tensor_scalar_add(R0[0:1, 18:19], R0[0:1, 8:9], 1.0), inc="V")
    WAIT("scalar", "V", vrc)
    OP("scalar", lambda e: e.activation(R0[0:1, 19:20], R0[0:1, 18:19], AF.Ln))
    OP("scalar", lambda e: e.activation(R0[0:1, 10:11], R0[0:1, 19:20], AF.Identity, scale=-1.0))
    OP("scalar", lambda e: e.activation(R0[0:1, 9:10], R0[0:1, 10:11], AF.Exp))
    OP("scalar", lambda e: e.activation(R0[0:1, 12:13], R0[0:1, 10:11], AF.Exp, scale=float(CP)))
    li1 = OP("sync", lambda e: e.dma_start(out=WFT.ap()[0:1, 0:CP], in_=IOTAd[0:1, 0:CP]), inc="L", n=16)
    WAIT("scalar", "L", li1)
    apa = OP("scalar", lambda e: e.activation(WFT.ap()[0:1, CP:2 * CP], WFT.ap()[0:1, 0:CP], AF.Exp, scale=R0[0:1, 10:11]), inc="A")
    WAIT("gpsimd", "A", apa)
    gd1 = OP("gpsimd", lambda e: e.dma_start(out=hrowD[:, :], in_=WFT.ap()[0:1, CP:2 * CP]), inc="G", n=16)
    WAIT("gpsimd", "G", gd1)
    gdec = OP("gpsimd", lambda e: e.dma_start(out=DEC[:, :], in_=hrowD.ap().broadcast_to([128, CP])), inc="G", n=16)
    WAIT("sync", "A", apa)
    li2 = OP("sync", lambda e: e.dma_start(out=WFT.ap()[0:1, 0:CP], in_=IOTAd[0:1, CP:2 * CP]), inc="L", n=16)
    WAIT("scalar", "L", li2)
    WAIT("scalar", "G", gd1)
    apw = OP("scalar", lambda e: e.activation(WFT.ap()[0:1, CP:2 * CP], WFT.ap()[0:1, 0:CP], AF.Exp, scale=R0[0:1, 10:11]), inc="A")
    WAIT("gpsimd", "A", apw)
    ggw = OP("gpsimd", lambda e: e.dma_start(out=hrowD[:, :], in_=WFT.ap()[0:1, CP:2 * CP]), inc="G", n=16)
    WAIT("gpsimd", "G", ggw)
    ev["gwc"] = OP("gpsimd", lambda e: e.dma_start(out=GWC[:, :], in_=hrowD.ap().rearrange("a (s p) -> (a p) s", p=128), allow_slow_non_contiguous=True), inc="G", n=16)
    gs1 = OP("gpsimd", lambda e: e.dma_start(out=hscal[0:1, 0:3], in_=R0[0:1, 10:13]), inc="G", n=16)
    WAIT("gpsimd", "G", gs1)
    for j, col in ((0, 20), (1, 21), (2, 22)):
        gs2 = OP("gpsimd", lambda e, j=j, col=col: e.dma_start(out=R0[:, col:col + 1], in_=hscal[0:1, j:j + 1].broadcast_to([128, 1])), inc="G", n=16)
    ev["attn_scal"] = gs2

    def ln_stats(var_scale, var_eps, xdeps, row0_free):
        evs = {"q_dma": [], "r_done": []}
        if row0_free is not None:
            WAIT("scalar", row0_free[0], row0_free[1])
            WAIT("vector", row0_free[0], row0_free[1])
        for tc in range(NCH):
            c0 = tc * CP
            if xdeps and xdeps[tc] is not None:
                WAIT("vector", xdeps[tc][0], xdeps[tc][1])
            vxb = OP("vector", lambda e, c0=c0: e.tensor_copy(RKS3[:, :, :], x3(0, K, c0, CP)), inc="V")
            WAIT("tensor", "V", vxb)
            for k in range(K):
                tsum = OP("tensor", lambda e, k=k: e.matmul(pS[:, :], ONE[:, :], RKS.ap()[:, CP * k:CP * (k + 1)], start=(k == 0), stop=(k == K - 1)), inc="T" if k == K - 1 else None)
            WAIT("scalar", "T", tsum)
            aS1 = OP("scalar", lambda e, c0=c0: e.activation(RBT.ap()[0:1, c0:c0 + CP], pS[:, :], AF.Identity), inc="A")
            WAIT("vector", "T", tsum)
            vsq = OP("vector", lambda e: e.tensor_mul(RKS3[:, :, :], RKS3[:, :, :], RKS3[:, :, :]), inc="V")
            WAIT("tensor", "V", vsq)
            WAIT("tensor", "A", aS1)
            for k in range(K):
                tsq = OP("tensor", lambda e, k=k: e.matmul(pS[:, :], ONE[:, :], RKS.ap()[:, CP * k:CP * (k + 1)], start=(k == 0), stop=(k == K - 1)), inc="T" if k == K - 1 else None)
            WAIT("scalar", "T", tsq)
            aS2 = OP("scalar", lambda e: e.activation(U1.ap()[0:1, 0:CP], pS[:, :], AF.Identity), inc="A")
            WAIT("scalar", "A", aS2)
            OP("scalar", lambda e, c0=c0: e.activation(U1.ap()[0:1, 512:1024], RBT.ap()[0:1, c0:c0 + CP], AF.Identity, scale=1.0 / D))
            asq = OP("scalar", lambda e: e.activation(U1.ap()[0:1, 1024:1536], U1.ap()[0:1, 512:1024], AF.Square, scale=float(np.sqrt(var_scale))), inc="A")
            WAIT("vector", "A", asq)
            vt_ = OP("vector", lambda e: e.scalar_tensor_tensor(U1.ap()[0:1, 1536:2048], U1.ap()[0:1, 0:CP], var_scale / D, U1.ap()[0:1, 1024:1536], ALU.mult, ALU.subtract), inc="V")
            vep = OP("vector", lambda e: e.tensor_scalar_add(U1.ap()[0:1, 1536:2048], U1.ap()[0:1, 1536:2048], float(var_eps)), inc="V")
            WAIT("scalar", "V", vep)
            OP("scalar", lambda e: e.activation(U1.ap()[0:1, 1024:1536], U1.ap()[0:1, 1536:2048], AF.Ln))
            ar = OP("scalar", lambda e, c0=c0: e.activation(RBT.ap()[0:1, c0:c0 + CP], U1.ap()[0:1, 1024:1536], AF.Exp, scale=-0.5), inc="A")
            evs["r_done"].append(ar)
            WAIT("vector", "A", ar)
            vq = OP("vector", lambda e, c0=c0: e.scalar_tensor_tensor(U1.ap()[0:1, 0:CP], U1.ap()[0:1, 512:1024], -1.0, RBT.ap()[0:1, c0:c0 + CP], ALU.mult, ALU.mult), inc="V")
            WAIT("gpsimd", "V", vq)
            gq = OP("gpsimd", lambda e, c0=c0: e.dma_start(out=hrowQ[0:1, c0:c0 + CP], in_=U1.ap()[0:1, 0:CP]), inc="G", n=16)
            evs["q_dma"].append(gq)
        WAIT("gpsimd", "A", evs["r_done"][-1])
        gr = OP("gpsimd", lambda e: e.dma_start(out=hrowR[:, :], in_=RBT.ap()[0:1, :]), inc="G", n=16)
        WAIT("gpsimd", "G", gr)
        evs["rbt"] = OP("gpsimd", lambda e: e.dma_start(out=RBT[:, :], in_=hrowR.ap().broadcast_to([128, T])), inc="G", n=16)
        return evs

    def ln_apply(stats, is_ffn, pmax_base, xdeps):
        evs = {"xn": [], "xacc": [], "mx": []}
        prev_a = None
        prev_g = None
        v1_ = None
        for tc in range(NCH):
            c0 = tc * CP
            half_a = []
            for h in range(2):
                k0 = 4 * h
                WAIT("vector", "G", stats["rbt"])
                if xdeps and xdeps[tc] is not None:
                    WAIT("vector", xdeps[tc][0], xdeps[tc][1])
                if prev_a is not None:
                    WAIT("vector", "A", prev_a)
                if prev_g is not None:
                    WAIT("vector", "G", prev_g)
                v1_ = OP("vector", lambda e, c0=c0, k0=k0: e.tensor_mul(U13[:, :, :], x3(k0, 4, c0, CP), bc(RBT.ap()[:, c0:c0 + CP], 4)), inc="V")
                WAIT("gpsimd", "V", v1_)
                WAIT("gpsimd", "G", stats["q_dma"][tc])
                gq = OP("gpsimd", lambda e, c0=c0: e.dma_start(out=U13[:, :, :], in_=hrowQ.ap()[0:1, c0:c0 + CP].unsqueeze(1).broadcast_to([128, 4, CP]), accum_op=ALU.add), inc="G", n=16)
                if is_ffn:
                    WAIT("scalar", "G", gq)
                    for kk in range(4):
                        k = k0 + kk
                        axn = OP("scalar", lambda e, k=k, kk=kk, c0=c0: e.activation(xnsl(k, c0, CP), U1.ap()[:, CP * kk:CP * (kk + 1)], AF.Identity, bias=B8.ap()[:, k:k + 1], scale=G8.ap()[:, k:k + 1]), inc="A")
                    WAIT("vector", "A", axn)
                    col = pmax_base + tc * 2 + h
                    OP("vector", lambda e, k0=k0, c0=c0, col=col: e.tensor_scalar(xn3(k0, 4, c0, CP), xn3(k0, 4, c0, CP), 1.0, None, ALU.mult, ALU.max, accum_out=PMX.ap()[:, col:col + 1]), inc=None)
                    vmx = OP("vector", lambda e, k0=k0, c0=c0, col=col + 8: e.tensor_scalar(xn3(k0, 4, c0, CP), xn3(k0, 4, c0, CP), 1.0, None, ALU.mult, ALU.min, accum_out=PMX.ap()[:, col:col + 1]), inc="V")
                    evs["mx"].append(vmx)
                    prev_a, prev_g = axn, gq
                else:
                    WAIT("scalar", "G", gq)
                    ark = OP("scalar", lambda e, k0=k0, c0=c0: e.activation(xn3(k0, 4, c0, CP), U13[:, :, :], AF.Identity), inc="A")
                    WAIT("gpsimd", "A", ark)
                    gx = OP("gpsimd", lambda e, k0=k0, c0=c0: e.dma_start(out=x3(k0, 4, c0, CP), in_=U13[:, :, :], accum_op=ALU.add), inc="G", n=16)
                    evs["xacc"].append(gx)
                    prev_a, prev_g = ark, gx
                    half_a.append(ark)
            evs["xn"].append(prev_a if not is_ffn else axn)
        evs["last_v"] = v1_
        evs["last_a"] = prev_a
        evs["last_g"] = prev_g
        return evs

    st1 = ln_stats(1.001, 1e-8, [("L", ev["x"][-1])] + [None] * 3, None)
    ap1 = ln_apply(st1, True, 0, None)

    tap("xn1", lambda: XN.ap(), [("sync", "V", ap1["mx"][-1])])
    tap("pmx0", lambda: PMX.ap(), [("sync", "V", ap1["mx"][-1])])
    WAIT("scalar", "V", ap1["mx"][-1])
    acp1 = OP("scalar", lambda e: e.copy(U1.ap()[:, 0:16], PMX.ap()[:, 0:16]), inc="A")
    WAIT("vector", "A", acp1)
    vmx1 = OP("vector", lambda e: e.reduce_max(R0[:, 29:30], U1.ap()[:, 0:16], axis=mybir.AxisListType.X, apply_absolute_value=True), inc="V")
    WAIT("scalar", "V", vmx1)
    vm2 = OP("scalar", lambda e: e.copy(LMX[:, :], R0[:, 29:30]), inc="A")
    WAIT("tensor", "A", vm2)
    tm1 = OP("tensor", lambda e: e.transpose(pT[0:1, 0:128], LMX[:, :], IDT[:, :]), inc="T")
    WAIT("scalar", "T", tm1)
    tm2 = OP("scalar", lambda e: e.copy(U1.ap()[0:1, 32:160], pT[0:1, 0:128]), inc="A")
    WAIT("vector", "A", tm2)
    vm3 = OP("vector", lambda e: e.reduce_max(R0[0:1, 15:16], U1.ap()[0:1, 32:160], axis=mybir.AxisListType.X), inc="V")
    WAIT("gpsimd", "V", vm3)
    gm1 = OP("gpsimd", lambda e: e.dma_start(out=cc_in1[:, :], in_=R0[0:1, 15:16]), inc="G", n=16)
    WAIT("gpsimd", "G", gm1)
    cc1 = OP("gpsimd", lambda e: e.collective_compute("AllReduce", ALU.max, replica_groups=[list(range(8))], ins=[cc_in1.ap()], outs=[cc_out1.ap()]), inc="C")

    von = OP("vector", lambda e: e.memset(R0[:, 28:29], 1.0), inc="V")
    bred = []
    for k in range(K):
        WAIT("sync", "A", apw)
        WAIT("sync", "G", ev["gwc"])
        if k > 0:
            WAIT("sync", "V", bred[-1])
        lwf = OP("sync", lambda e, k=k: e.dma_start(out=WFT[:, :], in_=wf[128 * k:128 * (k + 1), :]), inc="L", n=16)
        WAIT("vector", "L", lwf)
        vbr = OP("vector", lambda e, k=k: e.reduce_sum(PMX.ap()[:, 0 + k:1 + k], WFT[:, :], axis=mybir.AxisListType.X, apply_absolute_value=True), inc="V")
        bred.append(vbr)
    WAIT("tensor", "V", bred[-1])
    WAIT("tensor", "V", von)
    WAIT("tensor", "A", st1["r_done"][-1])
    tb1 = OP("tensor", lambda e: e.matmul(pS[0:1, 0:8], R0[:, 28:29], PMX.ap()[:, 0:8], start=True, stop=True), inc="T")
    WAIT("scalar", "T", tb1)
    ab0 = OP("scalar", lambda e: e.copy(U1.ap()[0:1, 160:168], pS[0:1, 0:8]), inc="A")
    WAIT("vector", "A", ab0)
    vb1 = OP("vector", lambda e: e.reduce_sum(U1.ap()[0:1, 168:169], U1.ap()[0:1, 160:168], axis=mybir.AxisListType.X), inc="V")
    WAIT("scalar", "V", vb1)
    ab1 = OP("scalar", lambda e: e.activation(R0[0:1, 13:14], U1.ap()[0:1, 168:169], AF.Identity, scale=1.0 / (D * D)), inc="A")
    WAIT("vector", "A", ab1)
    vb3 = OP("vector", lambda e: e.tensor_scalar_add(R0[0:1, 18:19], R0[0:1, 13:14], float(EPS)), inc="V")
    WAIT("scalar", "V", vb3)
    OP("scalar", lambda e: e.activation(R0[0:1, 19:20], R0[0:1, 18:19], AF.Ln))
    vb4 = OP("scalar", lambda e: e.activation(R0[0:1, 14:15], R0[0:1, 19:20], AF.Exp, scale=-1.0), inc="A")
    WAIT("gpsimd", "A", vb4)
    gb1 = OP("gpsimd", lambda e: e.dma_start(out=hscal[0:1, 3:4], in_=R0[0:1, 14:15]), inc="G", n=16)
    WAIT("gpsimd", "G", gb1)
    gb2 = OP("gpsimd", lambda e: e.dma_start(out=R0[:, 23:24], in_=hscal[0:1, 3:4].broadcast_to([128, 1])), inc="G", n=16)
    WAIT("scalar", "G", gb2)
    wq_vs = []
    wq_as = []
    for k in range(K):
        pb = pC if k % 2 == 0 else pD
        WAIT("sync", "V", bred[-1])
        if k > 0:
            WAIT("sync", "A", wq_as[-1])
        lwf = OP("sync", lambda e, k=k: e.dma_start(out=WFT[:, :], in_=wf[128 * k:128 * (k + 1), :]), inc="L", n=16)
        WAIT("scalar", "L", lwf)
        if k >= 2:
            WAIT("scalar", "V", wq_vs[k - 2])
        aq = OP("scalar", lambda e, pb=pb: e.activation(pb[:, :], WFT[:, :], AF.Identity, scale=R0[:, 23:24]), inc="A")
        wq_as.append(aq)
        WAIT("vector", "A", aq)
        OP("vector", lambda e, pb=pb: e.tensor_scalar(pb[:, :], pb[:, :], MAGIC, -MAGIC, ALU.add, ALU.add))
        OP("vector", lambda e, pb=pb: e.tensor_scalar(pb[:, :], pb[:, :], 1.0, None, ALU.min))
        vq2 = OP("vector", lambda e, k=k, pb=pb: e.tensor_scalar(WA.ap()[:, D * k:D * (k + 1)], pb[:, :], -1.0, None, ALU.max), inc="V")
        wq_vs.append(vq2)
    ev["wq"] = wq_vs[-1]
    WAIT("gpsimd", "V", ev["wq"])
    ev["wq_hbm"] = OP("gpsimd", lambda e: e.dma_start(out=wqh[:, :], in_=WA[:, :]), inc="G", n=16)

    mt_last = None
    for s in range(4):
        WAIT("sync", "A", ap1["last_a"])
        WAIT("sync", "G", ap1["last_g"])
        if mt_last is not None:
            WAIT("sync", "V", mt_last)
        lE = OP("sync", lambda e, s=s: e.dma_start(out=U1.ap()[:, 1024:1024 + CP], in_=ETd[128 * s:128 * (s + 1), :]), inc="L", n=16)
        lC = OP("sync", lambda e, s=s: e.dma_start(out=VT.ap()[:, 0:CP], in_=CTd[128 * s:128 * (s + 1), :]), inc="L", n=16)
        WAIT("scalar", "L", lE)
        WAIT("scalar", "L", lC)
        WAIT("scalar", "G", ev["attn_scal"])
        aE = OP("scalar", lambda e: e.activation(U1.ap()[:, 1024 + CP:1024 + 2 * CP], U1.ap()[:, 1024:1024 + CP], AF.Exp, scale=R0[:, 20:21]), inc="A")
        WAIT("vector", "A", aE)
        mt_last = OP("vector", lambda e, s=s: e.tensor_mul(MT.ap()[:, CP * s:CP * (s + 1)], U1.ap()[:, 1024 + CP:1024 + 2 * CP], VT.ap()[:, 0:CP]), inc="V")
    ev["mt"] = mt_last
    ev["stz"] = OP("vector", lambda e: e.memset(STB[:, :], 0.0), inc="V")

    WAIT("gpsimd", "C", cc1)
    gg1 = OP("gpsimd", lambda e: e.dma_start(out=R0[0:1, 15:16], in_=cc_out1[:, :]), inc="G", n=16)
    WAIT("scalar", "G", gg1)
    OP("scalar", lambda e: e.activation(R0[0:1, 19:20], R0[0:1, 15:16], AF.Ln))
    aqs = OP("scalar", lambda e: e.activation(R0[0:1, 16:17], R0[0:1, 19:20], AF.Exp, scale=-1.0), inc="A")
    WAIT("vector", "A", aqs)
    OP("vector", lambda e: e.tensor_scalar_mul(R0[0:1, 16:17], R0[0:1, 16:17], QB))
    vg4 = OP("vector", lambda e: e.scalar_tensor_tensor(R0[0:1, 17:18], R0[0:1, 15:16], 0.5 / QB, R0[0:1, 13:14], ALU.mult, ALU.mult), inc="V")
    WAIT("gpsimd", "V", vg4)
    gg2 = OP("gpsimd", lambda e: e.dma_start(out=hscal[0:1, 4:6], in_=R0[0:1, 16:18]), inc="G", n=16)
    WAIT("gpsimd", "G", gg2)
    OP("gpsimd", lambda e: e.dma_start(out=R0[:, 24:25], in_=hscal[0:1, 4:5].broadcast_to([128, 1])), inc="G", n=16)
    ev["g1scal"] = OP("gpsimd", lambda e: e.dma_start(out=R0[:, 25:26], in_=hscal[0:1, 5:6].broadcast_to([128, 1])), inc="G", n=16)

    def ffn_tail(wslot, qs_col, ysc_col, scal_ev, wq_dep, u1_free, out_store=False):
        evs = {"xffn": [], "store": [], "last_T": None}
        prev_ae = {}
        prev_gx = None
        for tc in range(NCH):
            c0 = tc * CP
            WAIT("vector", "G", scal_ev)
            OP("vector", lambda e, c0=c0: e.tensor_scalar(xn3(0, K, c0, CP), xn3(0, K, c0, CP), R0[:, qs_col:qs_col + 1], CLIPB, ALU.mult, ALU.min))
            vq2 = OP("vector", lambda e, c0=c0: e.tensor_scalar(xn3(0, K, c0, CP), xn3(0, K, c0, CP), -CLIPB, None, ALU.max), inc="V")
            WAIT("tensor", "V", vq2)
            if wq_dep is not None and tc == 0:
                WAIT("tensor", wq_dep[0], wq_dep[1])
            for u in range(K):
                pb = pA if u % 2 == 0 else pB
                if u % 2 in prev_ae and u >= 2:
                    WAIT("tensor", "A", prev_ae[u % 2])
                for k in range(K):
                    tmm = OP("tensor", lambda e, u=u, k=k, pb=pb, c0=c0: e.matmul(pb[:, :], wslot.ap()[:, D * k + 128 * u: D * k + 128 * u + 128], xnsl(k, c0, CP), start=(k == 0), stop=(k == K - 1)), inc="T" if k == K - 1 else None)
                WAIT("scalar", "T", tmm)
                if tc == 0 and u == 0 and u1_free is not None:
                    WAIT("scalar", u1_free[0], u1_free[1])
                if u % 4 == 0 and prev_gx is not None:
                    WAIT("scalar", "G", prev_gx)
                ae = OP("scalar", lambda e, u=u, pb=pb: e.activation(U1.ap()[:, CP * (u % 4):CP * (u % 4 + 1)], pb[:, :], AF.Identity, scale=R0[:, ysc_col:ysc_col + 1]), inc="A")
                prev_ae[u % 2] = ae
                if u % 4 == 3:
                    WAIT("gpsimd", "A", ae)
                    gx = OP("gpsimd", lambda e, u=u, c0=c0: e.dma_start(out=x3(u - 3, 4, c0, CP), in_=U13[:, :, :], accum_op=ALU.add), inc="G", n=16)
                    evs["xffn"].append(gx)
                    prev_gx = gx
            evs["last_T"] = cnt["T"]
            if out_store:
                WAIT("sync", "G", evs["xffn"][-1])
                for k in range(K):
                    sst = OP("sync", lambda e, k=k, c0=c0: e.dma_start(out=yT[128 * k:128 * (k + 1), c0:c0 + CP], in_=xsl(k, c0, CP)), inc="S", n=16)
                evs["store"].append(sst)
        return evs

    ffn1 = ffn_tail(WA, 24, 25, ev["g1scal"], ("V", ev["wq"]), None)

    tap("pmx1", lambda: PMX.ap(), [("sync", "A", ab1)])
    tap("x1", lambda: X.ap(), [("sync", "G", ffn1["xffn"][-1])])
    tap("scal1", lambda: SCAL.ap(), [("sync", "G", ffn1["xffn"][-1])])
    WAIT("gpsimd", "T", ffn1["last_T"])
    for k in range(K):
        ev["wrT"] = OP("gpsimd", lambda e, k=k: e.dma_start(out=WA.ap()[:, D * k:D * (k + 1)], in_=wrT[128 * k:128 * (k + 1), :]), inc="G", n=16)

    xdeps2 = [("G", ffn1["xffn"][2 * tc + 1]) for tc in range(NCH)]
    st2 = ln_stats(1.0, 1e-5, xdeps2, ("V", ap1["last_v"]))
    ap2 = ln_apply(st2, False, 0, xdeps2)

    tap("mt", lambda: MT.ap(), [("sync", "V", ev["mt"])])
    tap("rk", lambda: XN.ap(), [("sync", "A", ap2["xn"][-1]), ("sync", "G", ap2["xacc"][-1])])
    WAIT("tensor", "V", ev["stz"])
    WAIT("tensor", "V", ev["mt"])
    prev_axpy = None
    prev_y_last = None
    prev_wk_last = None
    prev_inter_T = None
    y_last_all = []
    for c in range(NCH):
        c0 = c * CP
        rk_ev = ap2["xn"][c]
        if serial:
            for _eng in ENGS:
                for _s in ("T", "V", "A", "GE"):
                    WAIT(_eng, _s, cnt[_s])
        WAIT("tensor", "A", rk_ev)
        if c == 0:
            WAIT("tensor", "G", ev["wwT"])
        if prev_axpy is not None:
            WAIT("tensor", "V", prev_axpy)
        v_evs = []
        for s in range(4):
            pb = pC if s % 2 == 0 else pD
            if s >= 2:
                WAIT("tensor", "A", v_evs[s - 2])
            for nh in range(2):
                for k in range(K):
                    tv = OP("tensor", lambda e, s=s, nh=nh, k=k, pb=pb, c0=c0: e.matmul(
                        pb[:, 512 * nh:512 * (nh + 1)],
                        XNap[:, XNW * k + 4 + c0 + 128 * s: XNW * k + 4 + c0 + 128 * s + 128],
                        WB.ap()[:, D * k + 512 * nh: D * k + 512 * nh + 512],
                        start=(k == 0), stop=(k == K - 1)), inc="T" if k == K - 1 else None)
            WAIT("scalar", "T", tv)
            av = OP("scalar", lambda e, s=s, pb=pb: e.activation(VT.ap()[:, D * s:D * (s + 1)], pb[:, :], AF.Identity), inc="A")
            v_evs.append(av)
        if prev_y_last is not None:
            WAIT("tensor", "V", prev_y_last)
        sm_evs = []
        for s in range(4):
            pb = pA if s % 2 == 0 else pB
            if s >= 2:
                WAIT("tensor", "V", sm_evs[s - 2])
            for k in range(K):
                tS = OP("tensor", lambda e, s=s, k=k, pb=pb, c0=c0: e.matmul(
                    pb[:, :],
                    XNap[:, XNW * k + 3 + c0 + 128 * s: XNW * k + 3 + c0 + 128 * s + 128],
                    xnsl(k, c0, CP),
                    start=(k == 0), stop=(k == K - 1)), inc="T" if k == K - 1 else None)
            WAIT("vector", "T", tS)
            vSM = OP("vector", lambda e, s=s, pb=pb: e.tensor_mul(SMB.ap()[:, CP * s:CP * (s + 1)], pb[:, :], MT.ap()[:, CP * s:CP * (s + 1)]), inc="V")
            sm_evs.append(vSM)
        if prev_wk_last is not None:
            WAIT("tensor", "A", prev_wk_last)
        wk_evs = []
        for s in range(4):
            if s >= 1:
                WAIT("tensor", "A", wk_evs[s - 1])
            for k in range(K):
                tw = OP("tensor", lambda e, s=s, k=k, c0=c0: e.transpose(
                    pT[:, 128 * k:128 * (k + 1)],
                    XNap[:, XNW * k + 3 + c0 + 128 * s: XNW * k + 3 + c0 + 128 * s + 128],
                    IDT[:, :]), inc="T" if k == K - 1 else None)
            WAIT("scalar", "T", tw)
            aw = OP("scalar", lambda e, s=s: e.activation(WKT.ap()[:, D * s:D * (s + 1)], pT[:, :], AF.Identity, scale=GWC.ap()[:, s:s + 1]), inc="A")
            wk_evs.append(aw)
        WAIT("vector", "A", rk_ev)
        if prev_inter_T is not None:
            WAIT("vector", "T", prev_inter_T)
        grks = OP("vector", lambda e, c0=c0: e.tensor_tensor(RKS3[:, :, :], xn3(0, K, c0, CP), bc(DEC.ap()[:, :], K), ALU.mult), inc="V")
        WAIT("tensor", "V", grks)
        WAIT("tensor", "V", sm_evs[-1])
        WAIT("tensor", "A", v_evs[-1])
        if prev_axpy is not None:
            WAIT("tensor", "V", prev_axpy)
        rd_evs = []
        for d in range(K):
            pb = pA if d % 2 == 0 else pB
            if d >= 2:
                WAIT("tensor", "A", rd_evs[d - 2])
            for s in range(4):
                OP("tensor", lambda e, d=d, s=s, pb=pb: e.matmul(
                    pb[:, :], VT.ap()[:, D * s + 128 * d: D * s + 128 * d + 128],
                    SMB.ap()[:, CP * s:CP * (s + 1)], start=(s == 0), stop=False))
            for k in range(K):
                tr = OP("tensor", lambda e, d=d, k=k, pb=pb: e.matmul(
                    pb[:, :], STB.ap()[:, D * k + 128 * d: D * k + 128 * d + 128],
                    RKS.ap()[:, CP * k:CP * (k + 1)], start=False, stop=(k == K - 1)), inc="T" if k == K - 1 else None)
            WAIT("scalar", "T", tr)
            ar_ = OP("scalar", lambda e, d=d, pb=pb: e.activation(RDS.ap()[:, CP * d:CP * (d + 1)], pb[:, :], AF.Identity), inc="A")
            rd_evs.append(ar_)
        if c == 0:
            tap("vt0", lambda: VT.ap(), [("sync", "A", v_evs[-1])])
            tap("smb0", lambda: SMB.ap(), [("sync", "V", sm_evs[-1])])
            tap("wkt0", lambda: WKT.ap(), [("sync", "A", wk_evs[-1])])
            tap("rds0", lambda: RDS.ap(), [("sync", "A", rd_evs[-1])])

        if c < NCH - 1:
            WAIT("tensor", "A", wk_evs[-1])
            ax_evs = []
            for eb in range(K):
                pb = pC if eb % 2 == 0 else pD
                if eb >= 2:
                    WAIT("tensor", "V", ax_evs[eb - 2])
                for s in range(4):
                    for nh in range(2):
                        td = OP("tensor", lambda e, eb=eb, s=s, nh=nh, pb=pb: e.matmul(
                            pb[:, 512 * nh:512 * (nh + 1)],
                            WKT.ap()[:, D * s + 128 * eb: D * s + 128 * eb + 128],
                            VT.ap()[:, D * s + 512 * nh: D * s + 512 * nh + 512],
                            start=(s == 0), stop=(s == 3)), inc="T" if (s == 3 and nh == 1) else None)
                WAIT("vector", "T", td)
                vax = OP("vector", lambda e, eb=eb, pb=pb: e.scalar_tensor_tensor(
                    STB.ap()[:, D * eb:D * (eb + 1)], STB.ap()[:, D * eb:D * (eb + 1)],
                    R0[:, 22:23], pb[:, :], ALU.mult, ALU.add), inc="V")
                ax_evs.append(vax)
            prev_axpy = ax_evs[-1]
            WAIT("tensor", "V", prev_axpy)
            if c == 0:
                tap("stb0", lambda: STB.ap(), [("sync", "V", ax_evs[-1])])
        if c == 0:
            WAIT("tensor", "G", ev["wrT"])
        WAIT("tensor", "A", rd_evs[-1])
        y_evs = []
        for u in range(K):
            pb = pA if u % 2 == 0 else pB
            if u >= 2:
                WAIT("tensor", "V", y_evs[u - 2])
            for d in range(K):
                ty = OP("tensor", lambda e, u=u, d=d, pb=pb: e.matmul(
                    pb[:, :], WA.ap()[:, D * d + 128 * u: D * d + 128 * u + 128],
                    RDS.ap()[:, CP * d:CP * (d + 1)], start=(d == 0), stop=(d == K - 1)), inc="T" if d == K - 1 else None)
            WAIT("vector", "T", ty)
            WAIT("vector", "G", ap2["xacc"][2 * c + 1])
            vy = OP("vector", lambda e, u=u, pb=pb, c0=c0: e.scalar_tensor_tensor(
                xsl(u, c0, CP), pb[:, :], R0[:, 21:22], xsl(u, c0, CP), ALU.mult, ALU.add), inc="V")
            y_evs.append(vy)
        y_last_all.append(y_evs[-1])
        prev_y_last = y_evs[-1]
        prev_wk_last = wk_evs[-1]
        if c == NCH - 1:
            WAIT("gpsimd", "A", v_evs[-1])
            WAIT("gpsimd", "G", ev["wq_hbm"])
            ev["wq2"] = OP("gpsimd", lambda e: e.dma_start(out=WB[:, :], in_=wqh[:, :]), inc="G", n=16)

    tap("x2", lambda: X.ap(), [("sync", "V", y_last_all[-1])])
    st3 = ln_stats(1.001, 1e-8, [("V", y_last_all[tc]) for tc in range(NCH)], ("V", ap2["last_v"]))
    ap3 = ln_apply(st3, True, 16, [("V", y_last_all[tc]) for tc in range(NCH)])

    WAIT("scalar", "V", ap3["mx"][-1])
    acp2 = OP("scalar", lambda e: e.copy(U1.ap()[:, 0:16], PMX.ap()[:, 16:32]), inc="A")
    WAIT("vector", "A", acp2)
    wmx1 = OP("vector", lambda e: e.reduce_max(R0[:, 29:30], U1.ap()[:, 0:16], axis=mybir.AxisListType.X, apply_absolute_value=True), inc="V")
    WAIT("scalar", "V", wmx1)
    wm2 = OP("scalar", lambda e: e.copy(LMX[:, :], R0[:, 29:30]), inc="A")
    WAIT("tensor", "A", wm2)
    wm3 = OP("tensor", lambda e: e.transpose(pT[0:1, 0:128], LMX[:, :], IDT[:, :]), inc="T")
    WAIT("scalar", "T", wm3)
    wm3b = OP("scalar", lambda e: e.copy(U1.ap()[0:1, 32:160], pT[0:1, 0:128]), inc="A")
    WAIT("vector", "A", wm3b)
    wm4 = OP("vector", lambda e: e.reduce_max(R0[0:1, 15:16], U1.ap()[0:1, 32:160], axis=mybir.AxisListType.X), inc="V")
    WAIT("gpsimd", "V", wm4)
    wg1 = OP("gpsimd", lambda e: e.dma_start(out=cc_in2[:, :], in_=R0[0:1, 15:16]), inc="G", n=16)
    WAIT("gpsimd", "G", wg1)
    cc2 = OP("gpsimd", lambda e: e.collective_compute("AllReduce", ALU.max, replica_groups=[list(range(8))], ins=[cc_in2.ap()], outs=[cc_out2.ap()]), inc="C")
    WAIT("gpsimd", "C", cc2)
    wg2 = OP("gpsimd", lambda e: e.dma_start(out=R0[0:1, 15:16], in_=cc_out2[:, :]), inc="G", n=16)
    WAIT("scalar", "G", wg2)
    OP("scalar", lambda e: e.activation(R0[0:1, 19:20], R0[0:1, 15:16], AF.Ln))
    aq2 = OP("scalar", lambda e: e.activation(R0[0:1, 16:17], R0[0:1, 19:20], AF.Exp, scale=-1.0), inc="A")
    WAIT("vector", "A", aq2)
    OP("vector", lambda e: e.tensor_scalar_mul(R0[0:1, 16:17], R0[0:1, 16:17], QB))
    wv4 = OP("vector", lambda e: e.scalar_tensor_tensor(R0[0:1, 17:18], R0[0:1, 15:16], 0.5 / QB, R0[0:1, 13:14], ALU.mult, ALU.mult), inc="V")
    WAIT("gpsimd", "V", wv4)
    wg3 = OP("gpsimd", lambda e: e.dma_start(out=hscal[0:1, 6:8], in_=R0[0:1, 16:18]), inc="G", n=16)
    WAIT("gpsimd", "G", wg3)
    OP("gpsimd", lambda e: e.dma_start(out=R0[:, 26:27], in_=hscal[0:1, 6:7].broadcast_to([128, 1])), inc="G", n=16)
    ev["g2scal"] = OP("gpsimd", lambda e: e.dma_start(out=R0[:, 27:28], in_=hscal[0:1, 7:8].broadcast_to([128, 1])), inc="G", n=16)

    ffn2 = ffn_tail(WB, 26, 27, ev["g2scal"], ("G", ev["wq2"]), None, out_store=True)

    tap("scal2", lambda: SCAL.ap(), [("sync", "G", cnt["G"])])
    WAIT("sync", "S", cnt["S"])

    blk = es.enter_context(nc.Block())
    blk.sync(lambda e: [fn(e) for fn in ops["sync"]])
    blk.gpsimd(lambda e: [fn(e) for fn in ops["gpsimd"]])
    blk.tensor(lambda e: [fn(e) for fn in ops["tensor"]])
    blk.vector(lambda e: [fn(e) for fn in ops["vector"]])
    blk.scalar(lambda e: [fn(e) for fn in ops["scalar"]])
    es.close()
    return nc


def stats_noop():
    return None


_NC_CACHE = None


def kernel(L0_attn_out, L0_attn_W_write, L0_attn_W_read, L0_attn_decay,
           L0_attn_log_alpha, L0_ffn_weight, L0_ffn_ln_gamma, L0_ffn_ln_beta):
    global _NC_CACHE
    if _NC_CACHE is None:
        _NC_CACHE = build_program()
    nc = _NC_CACHE

    x = np.asarray(L0_attn_out)
    B = x.shape[0]
    assert B == 8
    wwTh = np.ascontiguousarray(np.asarray(L0_attn_W_write).T)
    wrTh = np.ascontiguousarray(np.asarray(L0_attn_W_read).T)
    wfh = np.ascontiguousarray(np.asarray(L0_ffn_weight))
    g = np.ascontiguousarray(np.asarray(L0_ffn_ln_gamma)).reshape(1, D)
    b = np.ascontiguousarray(np.asarray(L0_ffn_ln_beta)).reshape(1, D)
    scal = np.zeros((1, 8), np.float32)
    scal[0, 0] = np.float32(L0_attn_decay)
    scal[0, 1] = np.float32(L0_attn_log_alpha)
    p = np.arange(CP)
    ETh = np.maximum(p[None, :] - 1 - p[:, None], 0).astype(np.float32)
    CTh = (p[None, :] > p[:, None]).astype(ml_dtypes.bfloat16)
    IOTAh = np.concatenate([p, CP - 1 - p]).astype(np.float32).reshape(1, 2 * CP)
    IDENTh = np.eye(128, dtype=ml_dtypes.bfloat16)
    ONESh = np.ones((128, 1), dtype=ml_dtypes.bfloat16)

    in_maps = []
    for i in range(B):
        in_maps.append(dict(
            xT=np.ascontiguousarray(x[i].T),
            wwT=wwTh, wrT=wrTh, wf=wfh, gvec=g, bvec=b, scal_in=scal,
            ETd=ETh, CTd=CTh, IOTAd=IOTAh, IDENTd=IDENTh, ONESd=ONESh,
        ))
    res = run_bass_kernel_spmd(nc, in_maps, core_ids=list(range(8)))
    out = np.stack([np.ascontiguousarray(np.asarray(r["yT"]).T) for r in res.results])
    return out.astype(np.float32)
